# version 21
# baseline (speedup 1.0000x reference)
from __future__ import annotations

import contextlib

import numpy as np
import ml_dtypes

import concourse.bass as bass
import concourse.mybir as mybir
from concourse.bass_utils import run_bass_kernel_spmd

F32 = mybir.dt.float32
BF16 = mybir.dt.bfloat16
I32 = mybir.dt.int32
ALU = mybir.AluOpType
ACT = mybir.ActivationFunctionType
AXX = mybir.AxisListType.X

B, S, D = 8, 2048, 1024
H, HD, FF = 16, 64, 4096
NT = S // 128
NG = S // 512
KC = D // 128
EPS = 1e-5


def _host_consts():
    import jax
    import jax.numpy as jnp
    with jax.default_device(jax.devices("cpu")[0]):
        inv_freq = 1.0 / 10000.0 ** (jnp.arange(0, 64, 2, dtype=jnp.float32) / 64)
        t = jnp.arange(S, dtype=jnp.float32)[:, None] * inv_freq[None, :]
        cosj = np.asarray(jnp.cos(t))
        sinj = np.asarray(jnp.sin(t))
    cos32 = cosj.T
    sin32 = sinj.T
    cos_tab = np.ascontiguousarray(np.tile(cos32, (4, 1)))
    sinsw = np.ascontiguousarray(
        np.concatenate([sin32, -sin32, sin32, -sin32], 0))
    cos_lo = np.zeros_like(cos_tab)
    sinsw_lo = np.zeros_like(sinsw)
    ident = np.eye(128, dtype=ml_dtypes.bfloat16)
    tri = (np.arange(128)[None, :] >= np.arange(128)[:, None]).astype(ml_dtypes.bfloat16)
    ones_c = np.ones((128, 128), dtype=ml_dtypes.bfloat16)
    return cos_tab, cos_lo, sinsw, sinsw_lo, ident, tri, ones_c


def _qkv_perm():
    return np.arange(3 * D, dtype=np.int64)


def _attn_schedule():
    pairs = []
    ks = 0
    for qc in range(NG):
        for hp in range(8):
            kts = []
            for kt in range(4 * qc + 4):
                qs = max(0, kt * 128 - qc * 512)
                diag = kt >= 4 * qc
                kts.append((kt, qs, diag, ks))
                ks += 1
            pairs.append((qc, hp, kts))
    nks = ks

    events = []
    pend_proj = []
    npairs = len(pairs)

    def drain_proj(n):
        for _ in range(min(n, len(pend_proj))):
            events.append(pend_proj.pop(0))

    for pi, (qc, hp, kts) in enumerate(pairs):
        n = len(kts)
        for j, (kt, qs, diag, k_s) in enumerate(kts):
            events.append(("qk", pi, kt, k_s, qs))
            if j == 1 and pi > 0:
                events.append(("rbc", pi - 1, 0))
                events.append(("rbc", pi - 1, 1))
            if j >= 1:
                pkt, pqs, pdiag, pks = kts[j - 1]
                events.append(("av", pi, pkt, pks, pqs, pkt == 0, pkt == n - 1))
        lk = kts[-1]
        events.append(("av", pi, lk[0], lk[3], lk[1], lk[0] == 0, True))
        if hp == 7:
            for tt in range(4 * qc, 4 * qc + 4):
                pend_proj.append(("proj", tt, 0, qc))
                pend_proj.append(("proj", tt, 1, qc))
    events.append(("rbc", npairs - 1, 0))
    events.append(("rbc", npairs - 1, 1))
    drain_proj(len(pend_proj))

    vec = []
    pend_ev = []
    for pi, (qc, hp, kts) in enumerate(pairs):
        for (kt, qs, diag, k_s) in kts:
            if diag:
                vec.append(("mask", k_s, qs))
        vec.append(("tail", pi))
        if hp == 7:
            for tt in range(4 * qc, 4 * qc + 4):
                pend_ev.append(("pevac", tt, 0))
                pend_ev.append(("pevac", tt, 1))
        take = 0 if pi < npairs - 1 else len(pend_ev)
        for _ in range(min(take, len(pend_ev))):
            vec.append(pend_ev.pop(0))
    while pend_ev:
        vec.append(pend_ev.pop(0))

    qk_done, av_done, rbc_done, prj_done = {}, {}, {}, {}
    t3 = t2 = 0
    for ev in events:
        if ev[0] == "qk":
            t3 += 1
            qk_done[ev[3]] = t3
        elif ev[0] == "av":
            t2 += 1
            av_done[ev[3]] = t2
        elif ev[0] == "rbc":
            t3 += 1
            rbc_done[(ev[1], ev[2])] = t3
        elif ev[0] == "proj":
            t3 += 1
            prj_done[(ev[1], ev[2])] = t3
    mask_v2, pevac_v2 = {}, {}
    v2 = 0
    for ev in vec:
        if ev[0] == "mask":
            v2 += 1
            mask_v2[ev[1]] = v2
        elif ev[0] == "pevac":
            v2 += 1
            pevac_v2[(ev[1], ev[2])] = v2
    sc_exp, sc_r = {}, {}
    scv = 0
    for pi, (qc, hp, kts) in enumerate(pairs):
        for (kt, qs, diag, k_s) in kts:
            scv += 1
            sc_exp[k_s] = scv
        scv += 1
        sc_r[pi] = scv
    return dict(pairs=pairs, events=events, vec=vec, qk_done=qk_done,
                av_done=av_done, rbc_done=rbc_done, prj_done=prj_done,
                mask_v2=mask_v2, pevac_v2=pevac_v2, sc_exp=sc_exp, sc_r=sc_r,
                nks=nks, v2_total=v2, sc_total=scv, t3_total=t3, t2_total=t2)


def build_nc():
    nc = bass.Bass("TRN2", target_bir_lowering=False,
                   detect_race_conditions=False)

    x_h = nc.dram_tensor("x", [S, D], F32, kind="ExternalInput")
    wqkv_h = nc.dram_tensor("w_qkv_p", [D, 3 * D], F32, kind="ExternalInput")
    wproj_h = nc.dram_tensor("w_proj", [D, D], F32, kind="ExternalInput")
    bproj_h = nc.dram_tensor("b_proj", [1, D], F32, kind="ExternalInput")
    gamma_h = nc.dram_tensor("ln_gamma", [1, D], F32, kind="ExternalInput")
    beta_h = nc.dram_tensor("ln_beta", [1, D], F32, kind="ExternalInput")
    fc1q_h = nc.dram_tensor("fc1_q", [D, FF], I32, kind="ExternalInput")
    fc1s_h = nc.dram_tensor("fc1_scales", [8, FF], F32, kind="ExternalInput")
    fc1b_h = nc.dram_tensor("fc1_bias", [1, FF], F32, kind="ExternalInput")
    fc2q_h = nc.dram_tensor("fc2_q", [FF, D], I32, kind="ExternalInput")
    fc2s_h = nc.dram_tensor("fc2_scales", [32, D], F32, kind="ExternalInput")
    fc2b_h = nc.dram_tensor("fc2_bias", [1, D], F32, kind="ExternalInput")
    cos_h = nc.dram_tensor("cos_tab", [128, S], F32, kind="ExternalInput")
    coslo_h = nc.dram_tensor("cos_lo", [128, S], F32, kind="ExternalInput")
    sinsw_h = nc.dram_tensor("sinsw_tab", [128, S], F32, kind="ExternalInput")
    sinswlo_h = nc.dram_tensor("sinsw_lo", [128, S], F32, kind="ExternalInput")
    ident_h = nc.dram_tensor("ident", [128, 128], BF16, kind="ExternalInput")
    tri_h = nc.dram_tensor("tri", [128, 128], BF16, kind="ExternalInput")
    ones_h = nc.dram_tensor("ones_c", [128, 128], BF16, kind="ExternalInput")

    out_h = nc.dram_tensor("out", [S, D], F32, kind="ExternalOutput")
    s_scr = nc.dram_tensor("s_scr", [S, D], BF16, kind="Internal")
    h_scr = nc.dram_tensor("h_scr", [KC, 128, S], BF16, kind="Internal")
    a2_scr = nc.dram_tensor("a2_scr", [NT, 128], F32, kind="Internal")

    es = contextlib.ExitStack()

    def sb(*a, **k):
        return es.enter_context(nc.sbuf_tensor(*a, **k))

    with es:
        a1c = sb("a1c", [128, NT], F32)
        b1c = sb("b1c", [128, NT], F32)
        a2c = sb("a2c", [128, NT], F32)
        ebias = sb("ebias", [128, 1], F32)
        ident_s = sb("ident_s", [128, 128], BF16)
        tri_s = sb("tri_s", [128, 128], BF16)
        ones_s = sb("ones_s", [128, 128], BF16)
        gam_s = sb("gam_s", [128, KC], F32)
        bet_s = sb("bet_s", [128, KC], F32)
        fc1b_s = sb("fc1b_s", [128, 32], F32)

        sems = {}
        for name in ("LD", "GLD", "SC", "S2", "VS", "V2", "V3", "T2", "T3"):
            sems[name] = es.enter_context(nc.semaphore(name + "_sem"))
        cnt = {k: 0 for k in sems}
        sem = sems

        esR1 = contextlib.ExitStack()
        wqkv = esR1.enter_context(
            nc.sbuf_tensor("wqkv", [128, KC * 3072], BF16, side="right"))
        cos_s = esR1.enter_context(
            nc.sbuf_tensor("cos_s", [128, S], BF16, side="right"))
        sinsw_s = esR1.enter_context(
            nc.sbuf_tensor("sinsw_s", [128, S], BF16, side="right"))
        xn_fm = esR1.enter_context(
            nc.sbuf_tensor("xn_fm", [128, KC * S], BF16, side="right"))

        esL1 = contextlib.ExitStack()
        xn_tm = esL1.enter_context(nc.sbuf_tensor("xn_tm", [128, NT * 1024], BF16))
        statw = esL1.enter_context(nc.sbuf_tensor("statw", [128, 8 * NT], F32))
        sqscr = esL1.enter_context(nc.sbuf_tensor("sqscr", [128, 2048], F32))
        esL1x = contextlib.ExitStack()
        xstg = esL1x.enter_context(nc.sbuf_tensor("xstg", [128, 4 * 4096], F32))

        LD0, GLD0, SC0, VS0 = cnt["LD"], cnt["GLD"], cnt["SC"], cnt["VS"]
        xg_done = [LD0 + 16, LD0 + 32, LD0 + 128, LD0 + 144]
        esW1 = contextlib.ExitStack()
        warmp = esW1.enter_context(nc.psum_tensor("warmp", [128, 512], F32))

        with nc.Block() as blk:
            @blk.tensor
            def _(t):
                t.wait_ge(sem["GLD"], GLD0 + 16)
                for _ in range(180):
                    t.matmul(warmp[:, :], wqkv[0:128, 0:128], wqkv[:, 0:512],
                             start=True, stop=True)

            @blk.sync
            def _(sy):
                for g in range(NG):
                    sy.dma_start(
                        xstg[:, g * 4096:(g + 1) * 4096].rearrange(
                            "p (c d) -> p c d", c=4),
                        x_h[g * 512:(g + 1) * 512, :].rearrange(
                            "(c p) d -> p c d", p=128)).then_inc(sem["LD"], 16)
                    if g == 1:
                        for dst2, src2 in (
                            (ident_s[:, :], ident_h[:, :]),
                            (tri_s[:, :], tri_h[:, :]),
                            (ones_s[:, :], ones_h[:, :]),
                            (gam_s[:, :], gamma_h[0:1, :].rearrange(
                                "o (c p) -> p (o c)", p=128)),
                            (bet_s[:, :], beta_h[0:1, :].rearrange(
                                "o (c p) -> p (o c)", p=128)),
                        ):
                            sy.dma_start(dst2, src2,
                                         allow_slow_non_contiguous=True
                                         ).then_inc(sem["LD"], 16)
                cnt["LD"] = LD0 + 144

            @blk.gpsimd
            def _(g):
                g.memset(ebias[:, :], -8.0)
                for k in range(KC):
                    g.dma_start(wqkv[:, k * 3072:(k + 1) * 3072],
                                wqkv_h[k * 128:(k + 1) * 128, :]).then_inc(
                                    sem["GLD"], 16)
                g.dma_start(cos_s[:, :], cos_h[:, :]).then_inc(sem["GLD"], 16)
                g.dma_start(sinsw_s[:, :], sinsw_h[:, :]).then_inc(sem["GLD"], 16)
                cnt["GLD"] = GLD0 + 16 * KC + 32

            @blk.scalar
            def _(s):
                for tt in range(NT):
                    g = tt // 4
                    xt = xstg[:, tt * 1024:(tt + 1) * 1024]
                    s.wait_ge(sem["LD"], xg_done[g])
                    if tt >= 2:
                        s.wait_ge(sem["VS"], VS0 + tt - 1)
                    s.activation(sqscr[:, (tt % 2) * 1024:(tt % 2 + 1) * 1024],
                                 xt, ACT.Square).then_inc(sem["SC"], 1)
                for tt in range(NT):
                    w = statw[:, 8 * tt:8 * tt + 8]
                    s.wait_ge(sem["VS"], VS0 + NT + tt + 1)
                    s.activation(a1c[:, tt:tt + 1], w[:, 4:5], ACT.Sqrt)
                    s.activation(a2c[:, tt:tt + 1], w[:, 6:7],
                                 ACT.Sqrt).then_inc(sem["SC"], 1)
                cnt["SC"] = SC0 + 2 * NT

            @blk.vector
            def _(v):
                def W(tt):
                    return statw[:, 8 * tt:8 * tt + 8]
                for tt in range(NT):
                    xt = xstg[:, tt * 1024:(tt + 1) * 1024]
                    v.wait_ge(sem["SC"], SC0 + tt + 1)
                    v.reduce_sum(W(tt)[:, 0:1], xt, axis=AXX)
                    v.reduce_sum(W(tt)[:, 1:2],
                                 sqscr[:, (tt % 2) * 1024:(tt % 2 + 1) * 1024],
                                 axis=AXX).then_inc(sem["VS"], 1)
                for tt in range(NT):
                    v.tensor_scalar(W(tt)[:, 2:3], W(tt)[:, 0:1], 1.0 / D, None,
                                    ALU.mult)
                for tt in range(NT):
                    v.tensor_tensor(W(tt)[:, 3:4], W(tt)[:, 2:3], W(tt)[:, 2:3],
                                    ALU.mult)
                for tt in range(NT):
                    v.scalar_tensor_tensor(W(tt)[:, 3:4], W(tt)[:, 1:2], 1.0 / D,
                                           W(tt)[:, 3:4], ALU.mult, ALU.subtract)
                for tt in range(NT):
                    v.tensor_scalar(W(tt)[:, 5:6], W(tt)[:, 3:4], EPS, None,
                                    ALU.add)
                for tt in range(NT):
                    v.reciprocal(W(tt)[:, 4:5], W(tt)[:, 5:6])
                for tt in range(NT):
                    v.tensor_tensor(W(tt)[:, 5:6], W(tt)[:, 3:4], W(tt)[:, 4:5],
                                    ALU.mult)
                for tt in range(NT):
                    v.tensor_scalar(W(tt)[:, 5:6], W(tt)[:, 5:6], EPS, None,
                                    ALU.add)
                for tt in range(NT):
                    v.reciprocal(W(tt)[:, 6:7], W(tt)[:, 5:6]).then_inc(
                        sem["VS"], 1)
                for tt in range(NT):
                    v.wait_ge(sem["SC"], SC0 + NT + tt + 1)
                    v.scalar_tensor_tensor(b1c[:, tt:tt + 1], W(tt)[:, 2:3], -1.0,
                                           a1c[:, tt:tt + 1], ALU.mult, ALU.mult)
                for tt in range(NT):
                    xt = xstg[:, tt * 1024:(tt + 1) * 1024]
                    v.tensor_scalar(xn_tm[:, tt * 1024:(tt + 1) * 1024], xt,
                                    a1c[:, tt:tt + 1], b1c[:, tt:tt + 1],
                                    ALU.mult, ALU.add)
                cnt["VS"] = VS0 + 2 * NT

        esL1x.close()
        esW1.close()

        T2b, V2b, LDb, GLDb, V3b, S2b = (cnt["T2"], cnt["V2"], cnt["LD"],
                                         cnt["GLD"], cnt["V3"], cnt["S2"])
        esB2 = contextlib.ExitStack()
        a2bc = esB2.enter_context(nc.sbuf_tensor("a2bc", [128, S], BF16))
        hstg = esB2.enter_context(nc.sbuf_tensor("hstg", [128, 2 * 2048], BF16))
        tp0 = esB2.enter_context(nc.psum_tensor("tp0", [128, 1024], BF16))
        tp1 = esB2.enter_context(nc.psum_tensor("tp1", [128, 1024], BF16))
        warmp2 = esB2.enter_context(nc.psum_tensor("warmp2", [128, 512], F32))

        with nc.Block() as blk:
            @blk.sync
            def _(sy):
                sy.dma_start(a2_scr[:, :].transpose([1, 0]), a2c[:, :],
                             allow_slow_non_contiguous=True
                             ).then_inc(sem["LD"], 16)
                cnt["LD"] = LDb + 16

            @blk.gpsimd
            def _(g):
                g.wait_ge(sem["LD"], LDb + 16)
                src = a2_scr[:, :].rearrange("t p -> (t p)").unsqueeze(0) \
                    .partition_broadcast(128)
                g.dma_start(a2bc[:, :], src).then_inc(sem["GLD"], 16)
                cnt["GLD"] = GLDb + 16

            @blk.tensor
            def _(t):
                t.wait_ge(sem["LD"], LD0 + 112)
                for idx in range(16):
                    c, hf = idx // 2, idx % 2
                    ps = tp0 if hf == 0 else tp1
                    if idx >= 2:
                        t.wait_ge(sem["V2"], V2b + idx - 1)
                    for j in range(8):
                        tt = hf * 8 + j
                        ins = t.transpose(
                            ps[:, j * 128:(j + 1) * 128],
                            xn_tm[:, tt * 1024 + c * 128:tt * 1024 + (c + 1) * 128],
                            ident_s[:, :])
                        if j == 7:
                            ins.then_inc(sem["T2"], 1)
                cnt["T2"] = T2b + 16

            @blk.vector
            def _(v):
                for idx in range(16):
                    c, hf = idx // 2, idx % 2
                    ps = tp0 if hf == 0 else tp1
                    v.wait_ge(sem["T2"], T2b + idx + 1)
                    v.tensor_copy(
                        xn_fm[:, c * S + hf * 1024:c * S + (hf + 1) * 1024],
                        ps[:, :]).then_inc(sem["V2"], 1)
                cnt["V2"] = V2b + 16

        with nc.Block() as blk:
            @blk.tensor
            def _(t):
                t.wait_ge(sem["GLD"], GLD0 + 16)
                for _ in range(55):
                    t.matmul(warmp2[:, :], wqkv[0:128, 0:128], wqkv[:, 0:512],
                             start=True, stop=True)

            @blk.vector
            def _(v):
                v.wait_ge(sem["GLD"], GLDb + 16)
                for c in range(KC):
                    if c >= 2:
                        v.wait_ge(sem["LD"], LDb + 16 + 16 * (c - 1))
                    v.tensor_tensor(hstg[:, (c % 2) * 2048:(c % 2 + 1) * 2048],
                                    xn_fm[:, c * S:(c + 1) * S],
                                    a2bc[:, :], ALU.mult).then_inc(sem["V3"], 1)
                cnt["V3"] = V3b + KC

            @blk.scalar
            def _(s):
                for c in range(KC):
                    s.wait_ge(sem["V3"], V3b + c + 1)
                    sl = hstg[:, (c % 2) * 2048:(c % 2 + 1) * 2048]
                    s.activation(sl, sl, ACT.Identity, bias=bet_s[:, c:c + 1],
                                 scale=gam_s[:, c:c + 1]).then_inc(sem["S2"], 1)
                cnt["S2"] = S2b + KC

            @blk.sync
            def _(sy):
                for c in range(KC):
                    sy.wait_ge(sem["S2"], S2b + c + 1)
                    sy.dma_start(h_scr[c, :, :],
                                 hstg[:, (c % 2) * 2048:(c % 2 + 1) * 2048]
                                 ).then_inc(sem["LD"], 16)
                cnt["LD"] = LDb + 16 + 16 * KC
        LD_hdone = LDb + 16 + 16 * KC

        esB2.close()
        esL1.close()

        esL2a = contextlib.ExitStack()
        Q_bf = esL2a.enter_context(nc.sbuf_tensor("Q_bf", [128, KC * S], BF16))
        K_bf = esL2a.enter_context(nc.sbuf_tensor("K_bf", [128, KC * S], BF16))
        V_tm = esL2a.enter_context(nc.sbuf_tensor("V_tm", [128, NT * H * 65], BF16))
        esL2b = contextlib.ExitStack()
        qraw = esL2b.enter_context(nc.sbuf_tensor("qraw", [128, 2 * 512], BF16))
        m1t = esL2b.enter_context(nc.sbuf_tensor("m1t", [128, 512], BF16))
        tbt = esL2b.enter_context(nc.sbuf_tensor("tbt", [128, 512], BF16))

        T3b, S2b, V3b = cnt["T3"], cnt["S2"], cnt["V3"]
        GLD_wq = GLD0 + 16 * KC
        esP3 = contextlib.ExitStack()
        qkp0 = esP3.enter_context(nc.psum_tensor("qkp0", [128, 2048], F32))
        qkp1 = esP3.enter_context(nc.psum_tensor("qkp1", [128, 2048], F32))

        with nc.Block() as blk:
            @blk.tensor
            def _(t):
                t.wait_ge(sem["GLD"], GLD_wq)
                for nci in range(16):
                    ps = qkp0 if nci % 2 == 0 else qkp1
                    if nci >= 2:
                        t.wait_ge(sem["S2"], S2b + 4 * (nci - 1))
                    col = nci * 128 if nci < 8 else 1024 + (nci - 8) * 128
                    for k in range(KC):
                        for g in range(4):
                            ins = t.matmul(
                                ps[:, g * 512:(g + 1) * 512],
                                wqkv[:, k * 3072 + col:k * 3072 + col + 128],
                                xn_fm[:, k * S + g * 512:k * S + (g + 1) * 512],
                                start=(k == 0), stop=(k == KC - 1))
                    ins.then_inc(sem["T3"], 1)
                cnt["T3"] = T3b + 16

            @blk.scalar
            def _(s):
                for nci in range(16):
                    ps = qkp0 if nci % 2 == 0 else qkp1
                    s.wait_ge(sem["T3"], T3b + nci + 1)
                    for g in range(4):
                        step = nci * 4 + g
                        if step >= 2:
                            s.wait_ge(sem["V3"], V3b + step - 1)
                        s.copy(qraw[:, (step % 2) * 512:(step % 2 + 1) * 512],
                               ps[:, g * 512:(g + 1) * 512]).then_inc(sem["S2"], 1)
                cnt["S2"] = S2b + 64

            @blk.vector
            def _(v):
                v.wait_ge(sem["LD"], LD_hdone)
                v.wait_ge(sem["GLD"], GLD0 + 16 * KC + 32)
                v.memset(V_tm[:, :].rearrange("p (t h c) -> p t h c",
                                              t=NT, h=H)[:, :, :, 64:65], 1.0)
                for nci in range(16):
                    for g in range(4):
                        step = nci * 4 + g
                        v.wait_ge(sem["S2"], S2b + step + 1)
                        qr = qraw[:, (step % 2) * 512:(step % 2 + 1) * 512]
                        gsl = slice(g * 512, (g + 1) * 512)
                        v.tensor_tensor(m1t[:, :], qr, cos_s[:, gsl], ALU.mult)
                        for blk in (0, 64):
                            v.tensor_tensor(tbt[blk:blk + 32, :],
                                            qr[blk + 32:blk + 64, :],
                                            sinsw_s[blk + 32:blk + 64, gsl],
                                            ALU.mult)
                            v.tensor_tensor(tbt[blk + 32:blk + 64, :],
                                            qr[blk:blk + 32, :],
                                            sinsw_s[blk:blk + 32, gsl],
                                            ALU.mult)
                        dst = Q_bf if nci < 8 else K_bf
                        c = nci if nci < 8 else nci - 8
                        v.tensor_tensor(dst[:, c * S + g * 512:c * S + (g + 1) * 512],
                                        m1t[:, :], tbt[:, :],
                                        ALU.add).then_inc(sem["V3"], 1)
                cnt["V3"] = V3b + 64

        esP3.close()

        T3b3, V3b3, S2b3 = cnt["T3"], cnt["V3"], cnt["S2"]
        esP3b = contextlib.ExitStack()
        vp0 = esP3b.enter_context(nc.psum_tensor("vp0", [128, 1024], F32))
        vp1 = esP3b.enter_context(nc.psum_tensor("vp1", [128, 1024], F32))

        with nc.Block() as blk:
            @blk.tensor
            def _(t):
                for tt in range(NT):
                    ps = vp0 if tt % 2 == 0 else vp1
                    if tt >= 2:
                        t.wait_ge(sem["V3"], V3b3 + tt - 1)
                    for k in range(KC):
                        for hf in range(2):
                            ins = t.matmul(
                                ps[:, hf * 512:(hf + 1) * 512],
                                xn_fm[:, k * S + tt * 128:k * S + (tt + 1) * 128],
                                wqkv[:, k * 3072 + 2048 + hf * 512:
                                     k * 3072 + 2048 + (hf + 1) * 512],
                                start=(k == 0), stop=(k == KC - 1))
                    ins.then_inc(sem["T3"], 1)
                cnt["T3"] = T3b3 + NT

            @blk.vector
            def _(v):
                for tt in range(NT):
                    ps = vp0 if tt % 2 == 0 else vp1
                    v.wait_ge(sem["T3"], T3b3 + tt + 1)
                    v.tensor_copy(
                        V_tm[:, tt * H * 65:tt * H * 65 + 520].rearrange(
                            "p (h c) -> p h c", h=8)[:, :, 0:64],
                        ps[:, 0:512].rearrange("p (h d) -> p h d", h=8))
                    v.tensor_copy(
                        V_tm[:, tt * H * 65 + 520:tt * H * 65 + 1040].rearrange(
                            "p (h c) -> p h c", h=8)[:, :, 0:64],
                        ps[:, 512:1024].rearrange("p (h d) -> p h d", h=8)
                    ).then_inc(sem["V3"], 1)
                cnt["V3"] = V3b3 + NT

        esP3b.close()
        esL2b.close()
        esR1.close()

        esR2 = contextlib.ExitStack()
        attn_fm = esR2.enter_context(
            nc.sbuf_tensor("attn_fm", [128, KC * S], BF16, side="right"))
        wproj = esR2.enter_context(
            nc.sbuf_tensor("wproj", [128, KC * 1024], BF16, side="right"))
        esR3 = contextlib.ExitStack()
        probs0 = esR3.enter_context(
            nc.sbuf_tensor("probs0", [128, 1024], BF16, side="right"))
        probs1 = esR3.enter_context(
            nc.sbuf_tensor("probs1", [128, 1024], BF16, side="right"))
        r_sb = esR3.enter_context(
            nc.sbuf_tensor("r_sb", [128, 1024], BF16, side="right"))
        rbcA = esR3.enter_context(
            nc.sbuf_tensor("rbcA", [64, 512], BF16, side="right"))
        rbcB = esR3.enter_context(
            nc.sbuf_tensor("rbcB", [64, 512], BF16, side="right"))
        lnbuf = esR3.enter_context(
            nc.sbuf_tensor("lnbuf", [128, 1024], F32, side="right"))
        sstg = esR3.enter_context(
            nc.sbuf_tensor("sstg", [128, 2 * 1024], BF16, side="right"))
        bpbc = esR3.enter_context(
            nc.sbuf_tensor("bpbc", [128, 1024], F32, side="right"))
        esP4 = contextlib.ExitStack()
        scp0 = esP4.enter_context(nc.psum_tensor("scp0", [128, 1024], F32))
        scp1 = esP4.enter_context(nc.psum_tensor("scp1", [128, 1024], F32))
        avp = esP4.enter_context(nc.psum_tensor("avp", [128, 1024], F32))
        rbp = esP4.enter_context(nc.psum_tensor("rbp", [128, 512], F32))
        prj = esP4.enter_context(nc.psum_tensor("prj", [128, 512], F32))

        sch = _attn_schedule()
        pairs = sch["pairs"]
        npairs = len(pairs)
        T3b4, T2b4, SCb4, V2b4, V3b4, GLDb4, LDb4 = (
            cnt["T3"], cnt["T2"], cnt["SC"], cnt["V2"], cnt["V3"],
            cnt["GLD"], cnt["LD"])
        diag_of_ks = {}
        for qc, hp, kts in pairs:
            for (kt, qs, diag, k_s) in kts:
                if diag:
                    diag_of_ks[k_s] = True

        with nc.Block() as blk:
            @blk.gpsimd
            def _(g):
                for k in range(KC):
                    g.dma_start(wproj[:, k * 1024:(k + 1) * 1024],
                                wproj_h[k * 128:(k + 1) * 128, :]).then_inc(
                                    sem["GLD"], 16)
                cnt["GLD"] = GLDb4 + 16 * KC

            @blk.sync
            def _(sy):
                sy.dma_start(bpbc[:, :],
                             bproj_h[0:1, :].partition_broadcast(128)
                             ).then_inc(sem["LD"], 16)
                for tt in range(NT):
                    sy.wait_ge(sem["V2"], V2b4 + sch["pevac_v2"][(tt, 1)])
                    sy.dma_start(s_scr[tt * 128:(tt + 1) * 128, :],
                                 sstg[:, (tt % 2) * 1024:(tt % 2 + 1) * 1024]
                                 ).then_inc(sem["LD"], 16)
                cnt["LD"] = LDb4 + 16 + 16 * NT

            @blk.tensor
            def _(t):
                prev_half = None
                first_proj = True
                for ev in sch["events"]:
                    if ev[0] == "qk":
                        _, pi, kt, k_s, qs = ev
                        qc, hp, kts = pairs[pi]
                        sc = scp0 if k_s % 2 == 0 else scp1
                        if k_s >= 2:
                            t.wait_ge(sem["SC"], SCb4 + sch["sc_exp"][k_s - 2])
                        ksl = slice(hp * S + kt * 128, hp * S + (kt + 1) * 128)
                        qsl = slice(hp * S + qc * 512 + qs, hp * S + (qc + 1) * 512)
                        t.matmul(sc[:, qs:512], K_bf[0:64, ksl], Q_bf[0:64, qsl],
                                 start=True, stop=True, tile_position=(0, 0))
                        t.matmul(sc[:, 512 + qs:1024], K_bf[64:128, ksl],
                                 Q_bf[64:128, qsl], start=True, stop=True,
                                 tile_position=(64, 0)).then_inc(sem["T3"], 1)
                    elif ev[0] == "av":
                        _, pi, kt, k_s, qs, first, last = ev
                        qc, hp, kts = pairs[pi]
                        pb = probs0 if k_s % 2 == 0 else probs1
                        t.wait_ge(sem["SC"], SCb4 + sch["sc_exp"][k_s])
                        if k_s in diag_of_ks:
                            t.wait_ge(sem["V2"], V2b4 + sch["mask_v2"][k_s])
                        if first:
                            t.wait_ge(sem["V3"], V3b4 + 3 * pi)
                        t.matmul(avp[0:65, qs:512],
                                 V_tm[:, kt * H * 65 + 2 * hp * 65:
                                      kt * H * 65 + 2 * hp * 65 + 65],
                                 pb[:, qs:512], start=first, stop=last)
                        t.matmul(avp[0:65, 512 + qs:1024],
                                 V_tm[:, kt * H * 65 + (2 * hp + 1) * 65:
                                      kt * H * 65 + (2 * hp + 1) * 65 + 65],
                                 pb[:, 512 + qs:1024], start=first,
                                 stop=last).then_inc(sem["T2"], 1)
                    elif ev[0] == "warm":
                        t.matmul(prj[0:64, 0:64], ones_s[0:64, 0:64],
                                 ones_s[0:64, 0:64], start=True, stop=True)
                    elif ev[0] == "rbc":
                        _, pi, which = ev
                        t.wait_ge(sem["SC"], SCb4 + sch["sc_r"][pi])
                        t.wait_ge(sem["V3"], V3b4 + 3 * pi + (1 if which else 0))
                        t.matmul(rbp[0:64, :], ones_s[64:65, 0:64],
                                 r_sb[64:65, which * 512:(which + 1) * 512],
                                 start=True, stop=True,
                                 tile_position=(64, 0)).then_inc(sem["T3"], 1)
                    else:
                        _, tt, hf, qc = ev
                        if first_proj:
                            t.wait_ge(sem["GLD"], GLDb4 + 16 * KC)
                            first_proj = False
                        t.wait_ge(sem["V3"], V3b4 + 3 * 8 * (qc + 1))
                        if prev_half is not None:
                            t.wait_ge(sem["V2"],
                                      V2b4 + sch["pevac_v2"][prev_half])
                        for c in range(KC):
                            ins = t.matmul(
                                prj[:, :],
                                attn_fm[:, c * S + tt * 128:c * S + (tt + 1) * 128],
                                wproj[:, c * 1024 + hf * 512:
                                      c * 1024 + (hf + 1) * 512],
                                start=(c == 0), stop=(c == KC - 1))
                        ins.then_inc(sem["T3"], 1)
                        prev_half = (tt, hf)
                cnt["T3"] = T3b4 + sch["t3_total"]
                cnt["T2"] = T2b4 + sch["t2_total"]

            @blk.scalar
            def _(s):
                for pi, (qc, hp, kts) in enumerate(pairs):
                    for (kt, qs, diag, k_s) in kts:
                        sc = scp0 if k_s % 2 == 0 else scp1
                        pb = probs0 if k_s % 2 == 0 else probs1
                        s.wait_ge(sem["T3"], T3b4 + sch["qk_done"][k_s])
                        if k_s >= 2:
                            s.wait_ge(sem["T2"], T2b4 + sch["av_done"][k_s - 2])
                        s.activation(
                            pb[:, :].rearrange("p (a b) -> p a b", a=2)[:, :, qs:512],
                            sc[:, :].rearrange("p (a b) -> p a b", a=2)[:, :, qs:512],
                            ACT.Exp, bias=ebias[:, 0:1],
                            scale=0.125).then_inc(sem["SC"], 1)
                    s.wait_ge(sem["T2"], T2b4 + sch["av_done"][kts[-1][3]])
                    if pi >= 1:
                        s.wait_ge(sem["T3"], T3b4 + sch["rbc_done"][(pi - 1, 1)])
                    s.activation(lnbuf[64:65, :], avp[64:65, :], ACT.Ln)
                    s.activation(r_sb[64:65, :], lnbuf[64:65, :], ACT.Exp,
                                 scale=-1.0).then_inc(sem["SC"], 1)
                cnt["SC"] = SCb4 + sch["sc_total"]

            @blk.vector
            def _(v):
                for ev in sch["vec"]:
                    if ev[0] == "mask":
                        _, k_s, qs = ev
                        pb = probs0 if k_s % 2 == 0 else probs1
                        v.wait_ge(sem["SC"], SCb4 + sch["sc_exp"][k_s])
                        v.tensor_tensor(pb[:, qs:qs + 128], pb[:, qs:qs + 128],
                                        tri_s[:, :], ALU.mult)
                        v.tensor_tensor(pb[:, 512 + qs:640 + qs],
                                        pb[:, 512 + qs:640 + qs],
                                        tri_s[:, :], ALU.mult).then_inc(sem["V2"], 1)
                    elif ev[0] == "tail":
                        _, pi = ev
                        qc, hp, kts = pairs[pi]
                        v.wait_ge(sem["T3"], T3b4 + sch["rbc_done"][(pi, 0)])
                        v.tensor_copy(rbcA[:, :], rbp[0:64, :]).then_inc(
                            sem["V3"], 1)
                        v.wait_ge(sem["T3"], T3b4 + sch["rbc_done"][(pi, 1)])
                        v.tensor_copy(rbcB[:, :], rbp[0:64, :])
                        v.tensor_tensor(
                            attn_fm[0:64, hp * S + qc * 512:hp * S + (qc + 1) * 512],
                            avp[0:64, 0:512], rbcA[:, :], ALU.mult)
                        v.tensor_tensor(
                            attn_fm[64:128, hp * S + qc * 512:hp * S + (qc + 1) * 512],
                            avp[0:64, 512:1024], rbcB[:, :],
                            ALU.mult).then_inc(sem["V3"], 2)
                    else:
                        _, tt, hf = ev
                        v.wait_ge(sem["T3"], T3b4 + sch["prj_done"][(tt, hf)])
                        v.wait_ge(sem["LD"], LDb4 + 16)
                        if tt >= 2 and hf == 0:
                            v.wait_ge(sem["LD"], LDb4 + 16 + 16 * (tt - 1))
                        v.tensor_tensor(
                            sstg[:, (tt % 2) * 1024 + hf * 512:
                                 (tt % 2) * 1024 + (hf + 1) * 512],
                            prj[:, :], bpbc[:, hf * 512:(hf + 1) * 512],
                            ALU.add).then_inc(sem["V2"], 1)
                cnt["V2"] = V2b4 + sch["v2_total"]
                cnt["V3"] = V3b4 + 3 * npairs

        esP4.close()
        esR3.close()
        esL2a.close()
        esR2.close()

        LDb6, GLDb6, S2b6, V3b6 = cnt["LD"], cnt["GLD"], cnt["S2"], cnt["V3"]
        esW = contextlib.ExitStack()
        fc1w = esW.enter_context(nc.sbuf_tensor("fc1w", [128, KC * FF], BF16))
        fc2w = esW.enter_context(nc.sbuf_tensor("fc2w", [128, 32 * 1024], BF16))
        esB6 = contextlib.ExitStack()
        prjw = esB6.enter_context(nc.psum_tensor("prjw", [128, 512], F32))
        istg = esB6.enter_context(
            nc.sbuf_tensor("istg", [128, 2 * 4096], I32, side="right"))
        scbc = esB6.enter_context(
            nc.sbuf_tensor("scbc", [128, 2 * 4096], BF16, side="right"))
        qmbf = esB6.enter_context(
            nc.sbuf_tensor("qmbf", [128, 2 * 4096], BF16, side="right"))

        scbc_done = {}
        istg_done = {}

        def _istg_src(j):
            if j < 8:
                return fc1q_h[j * 128:(j + 1) * 128, :]
            gg = j - 8
            return fc2q_h[gg * 512:(gg + 1) * 512, :].rearrange(
                "(c p) d -> p c d", p=128)

        def _istg_dst(j):
            sl = istg[:, (j % 2) * 4096:(j % 2 + 1) * 4096]
            return sl if j < 8 else sl.rearrange("p (c d) -> p c d", c=4)

        def _scbc_src(j):
            if j < 8:
                return fc1s_h[j:j + 1, :].partition_broadcast(128)
            gg = j - 8
            return fc2s_h[gg * 4:(gg + 1) * 4, :].rearrange(
                "a b -> (a b)").unsqueeze(0).partition_broadcast(128)

        with nc.Block() as blk:
            @blk.sync
            def _(sy):
                sy.wait_ge(sem["LD"], LDb6)
                lv = LDb6
                for j in range(16):
                    if j >= 2:
                        sy.wait_ge(sem["S2"], S2b6 + j - 1)
                    sy.dma_start(_istg_dst(j), _istg_src(j)).then_inc(
                        sem["LD"], 16)
                    lv += 16
                    istg_done[j] = ("LD", lv)
                cnt["LD"] = lv

            @blk.gpsimd
            def _(g):
                g.wait_ge(sem["LD"], LDb6)
                gv = GLDb6
                for j in range(16):
                    if j >= 2:
                        g.wait_ge(sem["V3"], V3b6 + j - 1)
                    g.dma_start(scbc[:, (j % 2) * 4096:(j % 2 + 1) * 4096],
                                _scbc_src(j)).then_inc(sem["GLD"], 16)
                    gv += 16
                    scbc_done[j] = ("GLD", gv)
                cnt["GLD"] = gv

            @blk.scalar
            def _(s):
                for j in range(16):
                    s.wait_ge(sem[istg_done[j][0]], istg_done[j][1])
                    if j >= 2:
                        s.wait_ge(sem["V3"], V3b6 + j - 1)
                    s.activation(qmbf[:, (j % 2) * 4096:(j % 2 + 1) * 4096],
                                 istg[:, (j % 2) * 4096:(j % 2 + 1) * 4096],
                                 ACT.Copy, bias=-128.0).then_inc(sem["S2"], 1)
                cnt["S2"] = S2b6 + 16

            @blk.tensor
            def _(t):
                for j in range(16):
                    t.wait_ge(sem["S2"], S2b6 + j + 1)
                    for _ in range(3):
                        t.matmul(prjw[0:64, 0:64], ones_s[0:64, 0:64],
                                 ones_s[0:64, 0:64], start=True, stop=True)

            @blk.vector
            def _(v):
                v.wait_ge(sem["LD"], LDb6)
                for j in range(16):
                    v.wait_ge(sem["S2"], S2b6 + j + 1)
                    v.wait_ge(sem[scbc_done[j][0]], scbc_done[j][1])
                    dst = fc1w[:, j * 4096:(j + 1) * 4096] if j < 8 else \
                        fc2w[:, (j - 8) * 4096:(j - 7) * 4096]
                    v.tensor_tensor(dst, qmbf[:, (j % 2) * 4096:(j % 2 + 1) * 4096],
                                    scbc[:, (j % 2) * 4096:(j % 2 + 1) * 4096],
                                    ALU.mult).then_inc(sem["V3"], 1)
                cnt["V3"] = V3b6 + 16

        esB6.close()

        LDb7, T3b7, T2b7, S2b7, SCb7, V3b7, GLDb7 = (
            cnt["LD"], cnt["T3"], cnt["T2"], cnt["S2"], cnt["SC"], cnt["V3"],
            cnt["GLD"])
        esB7 = contextlib.ExitStack()
        htile = esB7.enter_context(nc.sbuf_tensor("htile", [128, KC * 512], BF16))
        ybf = esB7.enter_context(nc.sbuf_tensor("ybf", [128, 32 * 512], BF16))
        xts = esB7.enter_context(nc.sbuf_tensor("xts", [128, 2 * 1024], F32))
        sts = esB7.enter_context(nc.sbuf_tensor("sts", [128, 2 * 1024], BF16))
        b2h = esB7.enter_context(nc.sbuf_tensor("b2h", [128, 1024], F32))
        uts = esB7.enter_context(nc.sbuf_tensor("uts", [128, 1024], F32))
        gts = esB7.enter_context(nc.sbuf_tensor("gts", [128, 1024], BF16))
        ostg = esB7.enter_context(nc.sbuf_tensor("ostg", [128, 1024], F32))
        yp = esB7.enter_context(nc.psum_tensor("yp", [128, 4 * 512], F32))
        zp0 = esB7.enter_context(nc.psum_tensor("zp0", [128, 1024], F32))
        zp1 = esB7.enter_context(nc.psum_tensor("zp1", [128, 1024], F32))

        def ld_b7(T, ts=None, what="h"):
            base = LDb7 + 32
            per_tile = 16 + 4 * 32
            if what == "h":
                return base + T * per_tile + 16
            return base + T * per_tile + 16 + ts * 32 + (16 if what == "x" else 32)

        with nc.Block() as blk:
            @blk.sync
            def _(sy):
                sy.dma_start(b2h[:, :],
                             fc2b_h[0:1, :].partition_broadcast(128)
                             ).then_inc(sem["LD"], 16)
                sy.dma_start(fc1b_s[:, :],
                             fc1b_h[0:1, :].rearrange("o (c p) -> p (o c)", p=128),
                             allow_slow_non_contiguous=True
                             ).then_inc(sem["LD"], 16)
                for T in range(4):
                    if T >= 1:
                        sy.wait_ge(sem["T3"], T3b7 + 32 * T)
                    hsrc = bass.AP(h_scr, T * 512,
                                   [[S, 128], [128 * S, KC], [1, 512]])
                    sy.dma_start(htile[:, :].rearrange("p (c f) -> p c f", c=KC),
                                 hsrc).then_inc(sem["LD"], 16)
                    for ts in range(4):
                        tg = T * 4 + ts
                        if tg >= 2:
                            sy.wait_ge(sem["V3"], V3b7 + 2 * (tg - 1))
                        sy.dma_start(xts[:, (ts % 2) * 1024:(ts % 2 + 1) * 1024],
                                     x_h[tg * 128:(tg + 1) * 128, :]
                                     ).then_inc(sem["LD"], 16)
                        sy.dma_start(sts[:, (ts % 2) * 1024:(ts % 2 + 1) * 1024],
                                     s_scr[tg * 128:(tg + 1) * 128, :]
                                     ).then_inc(sem["LD"], 16)
                cnt["LD"] = LDb7 + 32 + 4 * (16 + 4 * 32)

            @blk.tensor
            def _(t):
                for T in range(4):
                    t.wait_ge(sem["LD"], ld_b7(T, what="h"))
                    for n in range(32):
                        gi = T * 32 + n
                        if gi >= 4:
                            t.wait_ge(sem["S2"], S2b7 + gi - 3)
                        for k in range(KC):
                            ins = t.matmul(
                                yp[:, (n % 4) * 512:(n % 4 + 1) * 512],
                                fc1w[:, k * 4096 + n * 128:k * 4096 + (n + 1) * 128],
                                htile[:, k * 512:(k + 1) * 512],
                                start=(k == 0), stop=(k == KC - 1))
                        ins.then_inc(sem["T3"], 1)
                    t.wait_ge(sem["S2"], S2b7 + 32 * (T + 1))
                    for ts in range(4):
                        tg = T * 4 + ts
                        zp = zp0 if tg % 2 == 0 else zp1
                        if tg >= 2:
                            t.wait_ge(sem["V3"], V3b7 + 2 * (tg - 1) + 1)
                        for n in range(32):
                            for hf in range(2):
                                ins = t.matmul(
                                    zp[:, hf * 512:(hf + 1) * 512],
                                    ybf[:, n * 512 + ts * 128:n * 512 + (ts + 1) * 128],
                                    fc2w[:, n * 1024 + hf * 512:
                                         n * 1024 + (hf + 1) * 512],
                                    start=(n == 0), stop=(n == 31))
                        ins.then_inc(sem["T2"], 1)
                cnt["T3"] = T3b7 + 128
                cnt["T2"] = T2b7 + 16

            @blk.scalar
            def _(s):
                s.wait_ge(sem["LD"], LDb7 + 32)
                for T in range(4):
                    for n in range(32):
                        if n == 0 and T >= 1:
                            s.wait_ge(sem["T2"], T2b7 + 4 * T)
                        s.wait_ge(sem["T3"], T3b7 + T * 32 + n + 1)
                        s.activation(ybf[:, n * 512:(n + 1) * 512],
                                     yp[:, (n % 4) * 512:(n % 4 + 1) * 512],
                                     ACT.Silu, bias=fc1b_s[:, n:n + 1]
                                     ).then_inc(sem["S2"], 1)
                    for ts in range(4):
                        tg = T * 4 + ts
                        s.wait_ge(sem["V3"], V3b7 + 2 * tg + 1)
                        s.activation(gts[:, :], uts[:, :],
                                     ACT.Silu).then_inc(sem["SC"], 1)
                cnt["S2"] = S2b7 + 128
                cnt["SC"] = SCb7 + 16

            @blk.vector
            def _(v):
                v.wait_ge(sem["LD"], LDb7 + 16)
                v.tensor_scalar(b2h[:, :], b2h[:, :], 0.5, None, ALU.mult)
                for T in range(4):
                    for ts in range(4):
                        tg = T * 4 + ts
                        zp = zp0 if tg % 2 == 0 else zp1
                        v.wait_ge(sem["T2"], T2b7 + tg + 1)
                        v.wait_ge(sem["LD"], ld_b7(T, ts, "s"))
                        if tg >= 1:
                            v.wait_ge(sem["GLD"], GLDb7 + 16 * tg)
                        xt = xts[:, (ts % 2) * 1024:(ts % 2 + 1) * 1024]
                        v.scalar_tensor_tensor(uts[:, :], zp[:, :], 0.5,
                                               b2h[:, :], ALU.mult, ALU.add)
                        v.scalar_tensor_tensor(uts[:, :], xt, a1c[:, tg:tg + 1],
                                               uts[:, :], ALU.mult, ALU.add)
                        v.tensor_scalar(uts[:, :], uts[:, :], b1c[:, tg:tg + 1],
                                        None, ALU.add).then_inc(sem["V3"], 1)
                        v.wait_ge(sem["SC"], SCb7 + tg + 1)
                        v.tensor_tensor(gts[:, :],
                                        sts[:, (ts % 2) * 1024:(ts % 2 + 1) * 1024],
                                        gts[:, :], ALU.mult)
                        v.tensor_tensor(ostg[:, :], xt, gts[:, :],
                                        ALU.add).then_inc(sem["V3"], 1)
                cnt["V3"] = V3b7 + 32

            @blk.gpsimd
            def _(g):
                for tg in range(16):
                    g.wait_ge(sem["V3"], V3b7 + 2 * tg + 2)
                    g.dma_start(out_h[tg * 128:(tg + 1) * 128, :],
                                ostg[:, :]).then_inc(sem["GLD"], 16)
                g.wait_ge(sem["GLD"], GLDb7 + 16 * 16)
                cnt["GLD"] = GLDb7 + 256

        esB7.close()
        esW.close()

    return nc


_CACHE = {}


def kernel(L0_attn_x, L0_attn_w_qkv, L0_attn_w_proj, L0_attn_b_proj,
           L0_ffn_ln_gamma, L0_ffn_ln_beta, L0_ffn_fc1_q, L0_ffn_fc1_scales,
           L0_ffn_fc1_bias, L0_ffn_fc2_q, L0_ffn_fc2_scales, L0_ffn_fc2_bias):
    if "nc" not in _CACHE:
        _CACHE["nc"] = build_nc()
        _CACHE["consts"] = _host_consts()
        _CACHE["perm"] = _qkv_perm()
    nc = _CACHE["nc"]
    cos_tab, cos_lo, sinsw, sinsw_lo, ident, tri, ones_c = _CACHE["consts"]
    perm = _CACHE["perm"]

    wqkv_p = np.ascontiguousarray(np.asarray(L0_attn_w_qkv)[:, perm])
    common = {
        "w_qkv_p": wqkv_p,
        "w_proj": np.ascontiguousarray(L0_attn_w_proj),
        "b_proj": np.ascontiguousarray(L0_attn_b_proj).reshape(1, D),
        "ln_gamma": np.ascontiguousarray(L0_ffn_ln_gamma).reshape(1, D),
        "ln_beta": np.ascontiguousarray(L0_ffn_ln_beta).reshape(1, D),
        "fc1_q": np.ascontiguousarray(L0_ffn_fc1_q),
        "fc1_scales": np.ascontiguousarray(L0_ffn_fc1_scales),
        "fc1_bias": np.ascontiguousarray(L0_ffn_fc1_bias).reshape(1, FF),
        "fc2_q": np.ascontiguousarray(L0_ffn_fc2_q),
        "fc2_scales": np.ascontiguousarray(L0_ffn_fc2_scales),
        "fc2_bias": np.ascontiguousarray(L0_ffn_fc2_bias).reshape(1, D),
        "cos_tab": cos_tab, "cos_lo": cos_lo,
        "sinsw_tab": sinsw, "sinsw_lo": sinsw_lo,
        "ident": ident, "tri": tri, "ones_c": ones_c,
    }
    x = np.ascontiguousarray(np.asarray(L0_attn_x, dtype=np.float32))
    in_maps = [dict(common, x=np.ascontiguousarray(x[c])) for c in range(B)]
    res = run_bass_kernel_spmd(nc, in_maps, core_ids=list(range(B)))
    out = np.stack([np.asarray(res.results[c]["out"]) for c in range(B)], axis=0)
    return out.astype(np.float32)
